# version 01
# baseline (speedup 1.0000x reference)
import numpy as np
import contextlib
import concourse.bass as bass
import concourse.mybir as mybir
from concourse.bass_utils import run_bass_kernel_spmd

F32 = mybir.dt.float32
BF16 = mybir.dt.bfloat16
I32 = mybir.dt.int32
U32 = mybir.dt.uint32
AF = mybir.ActivationFunctionType
ALU = mybir.AluOpType
AX = mybir.AxisListType
IOA = bass.IndirectOffsetOnAxis

NCORES = 8
B, T, D = 16, 2048, 1024
BPC = B // NCORES
R = BPC * T
NT = R // 128


class P:

    def __init__(self):
        self.steps = []

    def st(self, eng, fn, inc=1):
        self.steps.append((eng, fn, inc))

    def dma(self, out, in_, **kw):
        self.st('sync', lambda e, o=out, i=in_, k=kw: e.dma_start(out=o, in_=i, **k), 16)

    def dmag(self, out, in_, **kw):
        self.st('gpsimd', lambda e, o=out, i=in_, k=kw: e.dma_start(out=o, in_=i, **k), 16)

    def op(self, eng, name, *a, **kw):
        self.st(eng, lambda e, n=name, aa=a, kk=kw: getattr(e, n)(*aa, **kk), 1)

    def build(self, nc, g, blk, last_eng='sync'):
        waits = []
        tot = 0
        for _, _, inc in self.steps:
            waits.append(tot)
            tot += inc
        final = tot

        def mk(name):
            def body(e):
                for i, (en, fn, inc) in enumerate(self.steps):
                    if en != name:
                        continue
                    e.wait_ge(g, waits[i])
                    fn(e).then_inc(g, inc)
                if name == last_eng:
                    e.wait_ge(g, final)
            return body
        blk.sync(mk('sync'))
        blk.vector(mk('vector'))
        blk.scalar(mk('scalar'))
        blk.tensor(mk('tensor'))
        blk.gpsimd(mk('gpsimd'))


def emit_ln(p, x_tile, out_bf, s):
    if not s.get('_init'):
        p.op('vector', 'memset', s['eps'][:], 1e-5)
        s['_init'] = True
    p.op('vector', 'reduce_sum', s['c0'][:], x_tile[:], axis=AX.X)
    p.op('scalar', 'activation', out_bf, x_tile[:], AF.Square, accum_out=s['c1'][:])
    p.op('vector', 'tensor_scalar_mul', s['c2'][:], s['c0'][:], 1.0 / 1024)
    p.op('vector', 'tensor_scalar_mul', s['c3'][:], s['c1'][:], 1.0 / 1024)
    p.op('vector', 'tensor_mul', s['c4'][:], s['c2'][:], s['c2'][:])
    p.op('vector', 'tensor_sub', s['c5'][:], s['c3'][:], s['c4'][:])
    p.op('scalar', 'activation', s['c6'][:], s['c5'][:], AF.Sqrt, bias=s['eps'][:])
    p.op('vector', 'reciprocal', s['c7'][:], s['c6'][:])
    p.op('vector', 'tensor_scalar', out_bf, x_tile[:], scalar1=s['c2'][:],
         scalar2=s['c7'][:], op0=ALU.subtract, op1=ALU.mult)


def ln_scratch(nc, ctx, tag=""):
    s = {}
    s['eps'] = ctx.enter_context(nc.sbuf_tensor(f"lneps{tag}", [128, 1], F32))
    for i in range(8):
        s[f'c{i}'] = ctx.enter_context(nc.sbuf_tensor(f"lnc{i}{tag}", [128, 1], F32))
    return s


def rope_tables():
    inv_freq = 1.0 / (10000.0 ** (np.arange(0, 64, 2, dtype=np.float64) / 64))
    t = np.arange(T, dtype=np.float64)
    f = t[None, :] * inv_freq[:, None]
    f128 = np.tile(f, (4, 1))
    cos2 = np.cos(f128).astype(np.float32)
    sin2 = np.sin(f128).astype(np.float32)
    return cos2, sin2


def rot_matrix():
    M = np.zeros((64, 64), np.float32)
    for j in range(32):
        M[j, j + 32] = -1.0
        M[32 + j, j] = 1.0
    lhsT = np.zeros((128, 128), np.float32)
    lhsT[:64, :64] = M.T
    lhsT[64:, 64:] = M.T
    return lhsT



def build_l0_attn():
    nc = bass.Bass("TRN2", target_bir_lowering=False)
    x = nc.dram_tensor("x", [R, D], F32, kind="ExternalInput").ap()
    wdk = nc.dram_tensor("wdk", [D, 128], F32, kind="ExternalInput").ap()
    wdv = nc.dram_tensor("wdv", [D, 128], F32, kind="ExternalInput").ap()
    wuk = nc.dram_tensor("wuk", [128, D], F32, kind="ExternalInput").ap()
    wuv = nc.dram_tensor("wuv", [128, D], F32, kind="ExternalInput").ap()
    wo = nc.dram_tensor("wo", [D, D], F32, kind="ExternalInput").ap()
    bdk = nc.dram_tensor("bdk", [1, 128], F32, kind="ExternalInput").ap()
    bdv = nc.dram_tensor("bdv", [1, 128], F32, kind="ExternalInput").ap()
    buk = nc.dram_tensor("buk", [1, D], F32, kind="ExternalInput").ap()
    buv = nc.dram_tensor("buv", [1, D], F32, kind="ExternalInput").ap()
    bo = nc.dram_tensor("bo", [1, D], F32, kind="ExternalInput").ap()
    identf = nc.dram_tensor("identf", [128, 128], F32, kind="ExternalInput").ap()
    identb = nc.dram_tensor("identb", [128, 128], BF16, kind="ExternalInput").ap()
    rotm = nc.dram_tensor("rotm", [128, 128], BF16, kind="ExternalInput").ap()
    cos2d = nc.dram_tensor("cos2", [128, T], F32, kind="ExternalInput").ap()
    sin2d = nc.dram_tensor("sin2", [128, T], F32, kind="ExternalInput").ap()
    onesf = nc.dram_tensor("onesf", [1, 128], F32, kind="ExternalInput").ap()
    y = nc.dram_tensor("y", [R, D], F32, kind="ExternalOutput").ap()

    ctx = contextlib.ExitStack()
    with ctx:
        sb = lambda name, shape, dt: ctx.enter_context(nc.sbuf_tensor(name, shape, dt))
        wdk_b = sb("wdk_b", [128, 1024], BF16)
        wdv_b = sb("wdv_b", [128, 1024], BF16)
        wuk_b = sb("wuk_b", [128, 1024], BF16)
        wuv_b = sb("wuv_b", [128, 1024], BF16)
        wo_b = sb("wo_b", [128, 8192], BF16)
        ident_s = sb("ident_s", [128, 128], F32)
        identb_s = sb("identb_s", [128, 128], BF16)
        rotm_s = sb("rotm_s", [128, 128], BF16)
        cos_s = sb("cos_s", [128, T], F32)
        sin_s = sb("sin_s", [128, T], F32)
        ones_s = sb("ones_s", [1, 128], F32)
        row_s = sb("row_s", [1, 1024], F32)
        bdk_c = sb("bdk_c", [128, 1], F32)
        bdv_c = sb("bdv_c", [128, 1], F32)
        buk_c = sb("buk_c", [128, 8], F32)
        buv_bc = sb("buv_bc", [128, 1024], F32)
        bo_bc = sb("bo_bc", [128, 1024], F32)
        x_t = sb("x_t", [128, 1024], F32)
        u_bf = sb("u_bf", [128, 1024], BF16)
        uT = sb("uT", [128, 8 * T], BF16)
        kT = sb("kT", [128, 8 * T], BF16)
        cTk = sb("cTk", [128, T], BF16)
        cTv = sb("cTv", [128, T], BF16)
        v_ext = sb("v_ext", [128, 16 * 1040], BF16)
        PT = sb("PT", [128, 16 * 512], BF16)
        tmp1 = sb("tmp1", [128, 512], F32)
        tmp2 = sb("tmp2", [128, 512], F32)
        tmpb = sb("tmpb", [128, 512], BF16)
        rrow = sb("rrow", [1, 512], F32)
        bc_s = sb("bc_s", [64, 512], F32)
        outt = sb("outt", [128, 1024], F32)
        lns = ln_scratch(nc, ctx)
        psA = ctx.enter_context(nc.psum_tensor("psA", [128, 512], F32))
        psB = ctx.enter_context(nc.psum_tensor("psB", [128, 512], F32))
        psY = ctx.enter_context(nc.psum_tensor("psY", [128, 512], F32))
        psT = ctx.enter_context(nc.psum_tensor("psT", [128, 128], F32))
        psTb = ctx.enter_context(nc.psum_tensor("psTb", [128, 128], BF16))
        g = ctx.enter_context(nc.semaphore("g"))
        blk = ctx.enter_context(nc.Block())

        p = P()
        p.dma(ident_s[:], identf[:, :])
        p.dma(identb_s[:], identb[:, :])
        p.dma(rotm_s[:], rotm[:, :])
        p.dma(cos_s[:], cos2d[:, :])
        p.dma(sin_s[:], sin2d[:, :])
        p.dma(ones_s[:], onesf[:, :])
        for i in range(8):
            p.dma(x_t[:, 0:128], wdk[128 * i:128 * (i + 1), :])
            p.op('vector', 'tensor_copy', wdk_b[:, 128 * i:128 * (i + 1)], x_t[:, 0:128])
            p.dma(x_t[:, 0:128], wdv[128 * i:128 * (i + 1), :])
            p.op('vector', 'tensor_copy', wdv_b[:, 128 * i:128 * (i + 1)], x_t[:, 0:128])
        p.dma(x_t[:], wuk[:, :])
        p.op('vector', 'tensor_copy', wuk_b[:], x_t[:])
        p.dma(x_t[:], wuv[:, :])
        p.op('vector', 'tensor_copy', wuv_b[:], x_t[:])
        for i in range(8):
            p.dma(x_t[:], wo[128 * i:128 * (i + 1), :])
            p.op('vector', 'tensor_copy', wo_b[:, 1024 * i:1024 * (i + 1)], x_t[:])
        p.dma(row_s[0:1, 0:128], bdk[:, :])
        p.op('tensor', 'transpose', psT[:, 0:1], row_s[0:1, 0:128], ident_s[0:1, 0:1])
        p.op('vector', 'tensor_copy', bdk_c[:], psT[:, 0:1])
        p.dma(row_s[0:1, 0:128], bdv[:, :])
        p.op('tensor', 'transpose', psT[:, 0:1], row_s[0:1, 0:128], ident_s[0:1, 0:1])
        p.op('vector', 'tensor_copy', bdv_c[:], psT[:, 0:1])
        p.dma(row_s[:], buk[:, :])
        for i in range(8):
            p.op('tensor', 'transpose', psT[:, 0:1], row_s[0:1, 128 * i:128 * (i + 1)], ident_s[0:1, 0:1])
            p.op('vector', 'tensor_copy', buk_c[:, i:i + 1], psT[:, 0:1])
        p.dma(row_s[:], buv[:, :])
        for n in range(2):
            p.op('tensor', 'matmul', psA[:, :], ones_s[0:1, :], row_s[0:1, 512 * n:512 * (n + 1)],
                 start=True, stop=True)
            p.op('vector', 'tensor_copy', buv_bc[:, 512 * n:512 * (n + 1)], psA[:, :])
        p.dma(row_s[:], bo[:, :])
        for n in range(2):
            p.op('tensor', 'matmul', psA[:, :], ones_s[0:1, :], row_s[0:1, 512 * n:512 * (n + 1)],
                 start=True, stop=True)
            p.op('vector', 'tensor_copy', bo_bc[:, 512 * n:512 * (n + 1)], psA[:, :])

        for b in range(BPC):
            tb = b * T
            p.op('vector', 'memset', v_ext[:], 1.0)
            for i in range(16):
                g0 = tb + 128 * i
                p.dma(x_t[:], x[g0:g0 + 128, :])
                emit_ln(p, x_t, u_bf[:], lns)
                for dt in range(8):
                    p.op('tensor', 'transpose', psTb[:], u_bf[:, 128 * dt:128 * (dt + 1)], identb_s[:])
                    p.op('scalar', 'copy', uT[:, T * dt + 128 * i: T * dt + 128 * (i + 1)], psTb[:])
            for j in range(4):
                for dt in range(8):
                    p.op('tensor', 'matmul', psA[:, :], wdk_b[:, 128 * dt:128 * (dt + 1)],
                         uT[:, T * dt + 512 * j: T * dt + 512 * (j + 1)],
                         start=(dt == 0), stop=(dt == 7))
                p.op('vector', 'tensor_scalar', cTk[:, 512 * j:512 * (j + 1)], psA[:, :],
                     scalar1=bdk_c[:], scalar2=None, op0=ALU.add)
                for dt in range(8):
                    p.op('tensor', 'matmul', psA[:, :], wdv_b[:, 128 * dt:128 * (dt + 1)],
                         uT[:, T * dt + 512 * j: T * dt + 512 * (j + 1)],
                         start=(dt == 0), stop=(dt == 7))
                p.op('vector', 'tensor_scalar', cTv[:, 512 * j:512 * (j + 1)], psA[:, :],
                     scalar1=bdv_c[:], scalar2=None, op0=ALU.add)
            for dt in range(8):
                for j in range(4):
                    p.op('tensor', 'matmul', psA[:, :], wuk_b[:, 128 * dt:128 * (dt + 1)],
                         cTk[:, 512 * j:512 * (j + 1)], start=True, stop=True)
                    p.op('vector', 'tensor_scalar', kT[:, T * dt + 512 * j: T * dt + 512 * (j + 1)],
                         psA[:, :], scalar1=buk_c[:, dt:dt + 1], scalar2=None, op0=ALU.add)
            for i in range(16):
                for n in range(2):
                    p.op('tensor', 'matmul', psA[:, :], cTv[:, 128 * i:128 * (i + 1)],
                         wuv_b[:, 512 * n:512 * (n + 1)], start=True, stop=True)
                    p.op('vector', 'tensor_tensor', tmpb[:], psA[:, :],
                         buv_bc[:, 512 * n:512 * (n + 1)], op=ALU.add)
                    for hh in range(8):
                        h = 8 * n + hh
                        p.op('vector', 'tensor_copy',
                             v_ext[:, 1040 * i + 65 * h: 1040 * i + 65 * h + 64],
                             tmpb[:, 64 * hh:64 * hh + 64])
            for dt in range(8):
                for j in range(4):
                    c0 = T * dt + 512 * j
                    for (src, dst) in ((uT, uT), (kT, kT)):
                        p.op('tensor', 'matmul', psA[:, :], rotm_s[:], src[:, c0:c0 + 512],
                             start=True, stop=True)
                        p.op('vector', 'tensor_tensor', tmp1[:], psA[:, :],
                             sin_s[:, 512 * j:512 * (j + 1)], op=ALU.mult)
                        p.op('vector', 'tensor_tensor', tmp2[:], src[:, c0:c0 + 512],
                             cos_s[:, 512 * j:512 * (j + 1)], op=ALU.mult)
                        p.op('vector', 'tensor_tensor', dst[:, c0:c0 + 512], tmp1[:], tmp2[:],
                             op=ALU.add)
            for h in range(16):
                dt = h // 2
                po = 64 * (h % 2)
                for qc in range(4):
                    q0 = T * dt + 512 * qc
                    for kt in range(16):
                        p.op('tensor', 'matmul', psA[:, :],
                             kT[po:po + 64, T * dt + 128 * kt: T * dt + 128 * (kt + 1)],
                             uT[po:po + 64, q0:q0 + 512], start=True, stop=True)
                        p.op('scalar', 'activation', PT[:, 512 * kt:512 * (kt + 1)], psA[:, :],
                             AF.Exp, scale=0.125)
                    for kt in range(16):
                        p.op('tensor', 'matmul', psY[0:65, :],
                             v_ext[:, 1040 * kt + 65 * h: 1040 * kt + 65 * h + 65],
                             PT[:, 512 * kt:512 * (kt + 1)], start=(kt == 0), stop=(kt == 15))
                    p.op('vector', 'reciprocal', rrow[0:1, :], psY[64:65, :])
                    p.op('tensor', 'matmul', psB[0:64, :], ones_s[0:1, 0:64], rrow[0:1, :],
                         start=True, stop=True)
                    p.op('scalar', 'copy', bc_s[:], psB[0:64, :])
                    p.op('vector', 'tensor_tensor', uT[po:po + 64, T * dt + 512 * qc: T * dt + 512 * (qc + 1)],
                         psY[0:64, :], bc_s[:], op=ALU.mult)
            for i in range(16):
                g0 = tb + 128 * i
                p.dma(x_t[:], x[g0:g0 + 128, :])
                for n in range(2):
                    for dt in range(8):
                        p.op('tensor', 'matmul', psA[:, :],
                             uT[:, T * dt + 128 * i: T * dt + 128 * (i + 1)],
                             wo_b[:, 1024 * dt + 512 * n: 1024 * dt + 512 * (n + 1)],
                             start=(dt == 0), stop=(dt == 7))
                    p.op('vector', 'tensor_tensor', tmp1[:], psA[:, :],
                         bo_bc[:, 512 * n:512 * (n + 1)], op=ALU.add)
                    p.op('vector', 'tensor_tensor', outt[:, 512 * n:512 * (n + 1)], tmp1[:],
                         x_t[:, 512 * n:512 * (n + 1)], op=ALU.add)
                p.dma(y[g0:g0 + 128, :], outt[:])
        p.build(nc, g, blk)
    return nc


def l0_attn_inmaps(inputs, xfull):
    cos2, sin2 = rope_tables()
    rotm = rot_matrix()
    import ml_dtypes
    common = {
        "wdk": inputs["L0_attn_w_dk"], "wdv": inputs["L0_attn_w_dv"],
        "wuk": inputs["L0_attn_w_uk"], "wuv": inputs["L0_attn_w_uv"],
        "wo": inputs["L0_attn_w_o"],
        "bdk": inputs["L0_attn_b_dk"].reshape(1, -1),
        "bdv": inputs["L0_attn_b_dv"].reshape(1, -1),
        "buk": inputs["L0_attn_b_uk"].reshape(1, -1),
        "buv": inputs["L0_attn_b_uv"].reshape(1, -1),
        "bo": inputs["L0_attn_b_o"].reshape(1, -1),
        "identf": np.eye(128, dtype=np.float32),
        "identb": np.eye(128, dtype=np.float32).astype(ml_dtypes.bfloat16),
        "rotm": rotm.astype(ml_dtypes.bfloat16),
        "cos2": cos2, "sin2": sin2,
        "onesf": np.ones((1, 128), np.float32),
    }
    maps = []
    for c in range(NCORES):
        m = dict(common)
        m["x"] = np.ascontiguousarray(xfull[c * R:(c + 1) * R])
        maps.append(m)
    return maps


def run_launch(nc, maps):
    res = run_bass_kernel_spmd(nc, maps, core_ids=list(range(NCORES)))
    return res.results


def kernel(**inputs):
    import ml_dtypes
    xfull = np.ascontiguousarray(inputs["L0_attn_x"].reshape(B * T, D).astype(np.float32))
    nc1 = build_l0_attn()
    res = run_launch(nc1, l0_attn_inmaps(inputs, xfull))
    x1 = np.concatenate([r["y"] for r in res], axis=0)
    nc2 = build_l0_ffn()
    res = run_launch(nc2, l0_ffn_inmaps(inputs, x1))
    x2 = np.concatenate([r["y"] for r in res], axis=0)
    nc3 = build_l1_attn()
    res = run_launch(nc3, l1_attn_inmaps(inputs, x2))
    x3 = np.concatenate([r["y"] for r in res], axis=0)
    nc4 = build_l1_route()
    res = run_launch(nc4, l1_ffn_inmaps_route(inputs, x3))
    asgT_all = np.concatenate([r["asgT"] for r in res], axis=1)
    nc5 = build_l1_prep()
    res = run_launch(nc5, l1_ffn_inmaps_prep(inputs, x3, asgT_all))
    ysh = [r["ysh"] for r in res]
    ub = [r["ubo"] for r in res]
    slots = [r["slotso"] for r in res]
    wsv = [r["wso"] for r in res]
    xs_parts = [[] for _ in range(NCORES)]
    for tiles in SCAT_GROUPS:
        ncs = build_l1_scat(tiles)
        maps = [{"ub": ub[c], "slotsi": slots[c]} for c in range(NCORES)]
        res = run_launch(ncs, maps)
        for c in range(NCORES):
            xs_parts[c].append(res[c]["XS"])
    nce = build_l1_exp(len(SCAT_GROUPS))
    common_e = {
        "w1e": inputs["L1_ffn_w1"].reshape(64 * 1024, 512),
        "w2e": inputs["L1_ffn_w2"].reshape(64 * 512, 1024),
        "identb": np.eye(128, dtype=np.float32).astype(ml_dtypes.bfloat16),
    }
    maps = []
    for c in range(NCORES):
        m = dict(common_e)
        for j in range(len(SCAT_GROUPS)):
            m[f"XS{j}"] = xs_parts[c][j]
        maps.append(m)
    res = run_launch(nce, maps)
    ys = [r["YS"] for r in res]
    out = np.zeros((B * T, D), np.float32)
    for tiles in SCAT_GROUPS:
        ncg = build_l1_gath(tiles)
        maps = [{"YS": ys[c], "slotsi": slots[c], "wsi": wsv[c], "ysh": ysh[c]}
                for c in range(NCORES)]
        res = run_launch(ncg, maps)
        for c in range(NCORES):
            for oi, i in enumerate(tiles):
                out[c * R + 128 * i: c * R + 128 * (i + 1)] = res[c]["y"][128 * oi:128 * (oi + 1)]
    return out.reshape(B, T, D)



def build_l0_ffn():
    nc = bass.Bass("TRN2", target_bir_lowering=False)
    x = nc.dram_tensor("x", [R, D], F32, kind="ExternalInput").ap()
    w1 = nc.dram_tensor("w1", [4096, D], F32, kind="ExternalInput").ap()
    w2 = nc.dram_tensor("w2", [D, 4096], F32, kind="ExternalInput").ap()
    convw = nc.dram_tensor("convw", [4096, 3], F32, kind="ExternalInput").ap()
    convb = nc.dram_tensor("convb", [4096, 1], F32, kind="ExternalInput").ap()
    identf = nc.dram_tensor("identf", [128, 128], F32, kind="ExternalInput").ap()
    identb = nc.dram_tensor("identb", [128, 128], BF16, kind="ExternalInput").ap()
    onesr = nc.dram_tensor("onesr", [1, 128], F32, kind="ExternalInput").ap()
    onesc = nc.dram_tensor("onesc", [128, 1], F32, kind="ExternalInput").ap()
    y = nc.dram_tensor("y", [R, D], F32, kind="ExternalOutput").ap()
    hsil = nc.dram_tensor("hsil", [4096, R], BF16, kind="Internal").ap()
    hq = nc.dram_tensor("hqd", [4096, R], BF16, kind="Internal").ap()

    ctx = contextlib.ExitStack()
    with ctx:
        sb = lambda name, shape, dt: ctx.enter_context(nc.sbuf_tensor(name, shape, dt))
        big1 = sb("big1", [128, 32768], BF16)
        big2 = sb("big2", [128, 32768], BF16)
        hf1 = sb("hf1", [128, 4096], F32)
        hf2 = sb("hf2", [128, 4096], BF16)
        hf3 = sb("hf3", [128, 4096], BF16)
        hf5 = sb("hf5", [128, 4096], F32)
        ident_s = sb("ident_s", [128, 128], F32)
        identb_s = sb("identb_s", [128, 128], BF16)
        onesr_s = sb("onesr_s", [1, 128], F32)
        onesrb_s = sb("onesrb_s", [1, 128], BF16)
        onesc_s = sb("onesc_s", [128, 1], F32)
        x_t = sb("x_t", [128, 1024], F32)
        u_f = sb("u_f", [128, 1024], F32)
        ti32 = sb("ti32", [128, 1024], I32)
        qbf = sb("qbf", [128, 4096], BF16)
        hqt_s = sb("hqt_s", [128, 128], BF16)
        cw_t = sb("cw_t", [128, 3], F32)
        cb_t = sb("cb_t", [128, 1], F32)
        invrow1 = sb("invrow1", [1, R], BF16)
        i2m2 = sb("i2m2", [128, 32], F32)
        acc = sb("acc", [128, 1], F32)
        c0 = sb("c0", [128, 1], F32)
        c1 = sb("c1", [128, 1], F32)
        c2 = sb("c2", [128, 1], F32)
        s1 = sb("s1", [1, 1], F32)
        s2 = sb("s2", [1, 1], F32)
        m1_col = sb("m1_col", [128, 1], F32)
        sc1w_col = sb("sc1w_col", [128, 1], F32)
        m2_col = sb("m2_col", [128, 1], F32)
        sc2w_col = sb("sc2w_col", [128, 1], F32)
        lns = ln_scratch(nc, ctx)
        psA = ctx.enter_context(nc.psum_tensor("psA", [128, 512], F32))
        psT = ctx.enter_context(nc.psum_tensor("psT", [128, 128], F32))
        psTb = ctx.enter_context(nc.psum_tensor("psTb", [128, 128], BF16))
        psC = ctx.enter_context(nc.psum_tensor("psC", [128, 1], F32))
        g = ctx.enter_context(nc.semaphore("g"))
        blk = ctx.enter_context(nc.Block())
        p = P()

        p.dma(ident_s[:], identf[:, :])
        p.dma(identb_s[:], identb[:, :])
        p.dma(onesr_s[:], onesr[:, :])
        p.op('vector', 'tensor_copy', onesrb_s[:], onesr_s[:])
        p.dma(onesc_s[:], onesc[:, :])

        def wstats(w_ap, ntiles, width, loadt, s_t, m_col, sc_col, nelem):
            p.op('vector', 'memset', acc[:], 0.0)
            for i in range(ntiles):
                p.dma(loadt[:, 0:width], w_ap[128 * i:128 * (i + 1), :])
                p.op('vector', 'reduce_sum', c0[:], loadt[:, 0:width], axis=AX.X,
                     apply_absolute_value=True)
                p.op('vector', 'tensor_tensor', acc[:], acc[:], c0[:], op=ALU.add)
            p.op('tensor', 'matmul', psC[0:1, 0:1], onesc_s[:], acc[:], start=True, stop=True)
            p.op('vector', 'tensor_scalar', s_t[:], psC[0:1, 0:1], scalar1=1.0 / nelem,
                 scalar2=1e-5, op0=ALU.mult, op1=ALU.max)
            p.op('tensor', 'matmul', psT[:, 0:1], onesr_s[0:1, :], s_t[0:1, 0:1], start=True, stop=True)
            p.op('vector', 'tensor_copy', m_col[:], psT[:, 0:1])
            p.op('vector', 'reciprocal', sc_col[:], m_col[:])
        wstats(w1, 32, 1024, x_t, s1, m1_col, sc1w_col, 4096.0 * 1024)
        wstats(w2, 8, 4096, hf1, s2, m2_col, sc2w_col, 4096.0 * 1024)

        for i in range(32):
            p.dma(x_t[:], w1[128 * i:128 * (i + 1), :])
            p.op('vector', 'tensor_scalar', hf5[:, 0:1024], x_t[:], scalar1=sc1w_col[:],
                 scalar2=None, op0=ALU.mult)
            p.op('vector', 'tensor_copy', ti32[:], hf5[:, 0:1024])
            p.op('vector', 'tensor_scalar', qbf[:, 0:1024], ti32[:], scalar1=-1.0,
                 scalar2=1.0, op0=ALU.max, op1=ALU.min)
            for dt in range(8):
                p.op('tensor', 'transpose', psTb[:], qbf[:, 128 * dt:128 * (dt + 1)], identb_s[:])
                p.op('scalar', 'copy', big1[:, 4096 * dt + 128 * i: 4096 * dt + 128 * (i + 1)], psTb[:])

        for i in range(32):
            p.dma(x_t[:], x[128 * i:128 * (i + 1), :])
            emit_ln(p, x_t, u_f[:], lns)
            p.op('vector', 'reduce_max', c0[:], u_f[:], axis=AX.X, apply_absolute_value=True)
            p.op('vector', 'tensor_scalar_max', c1[:], c0[:], 1e-5)
            p.op('vector', 'reciprocal', c2[:], c1[:])
            p.op('vector', 'tensor_scalar_mul', c2[:], c2[:], 127.0)
            p.op('vector', 'tensor_scalar', hf5[:, 0:1024], u_f[:], scalar1=c2[:], scalar2=None,
                 op0=ALU.mult)
            p.op('vector', 'tensor_copy', ti32[:], hf5[:, 0:1024])
            p.op('vector', 'tensor_scalar', qbf[:, 0:1024], ti32[:], scalar1=-128.0,
                 scalar2=127.0, op0=ALU.max, op1=ALU.min)
            p.op('vector', 'tensor_scalar_mul', c1[:], c1[:], 1.0 / 127)
            p.op('tensor', 'transpose', psT[0:1, 0:128], c1[:], ident_s[:])
            p.op('scalar', 'copy', invrow1[0:1, 128 * i:128 * (i + 1)], psT[0:1, 0:128])
            for dt in range(8):
                p.op('tensor', 'transpose', psTb[:], qbf[:, 128 * dt:128 * (dt + 1)], identb_s[:])
                p.op('scalar', 'copy', big2[:, 4096 * dt + 128 * i: 4096 * dt + 128 * (i + 1)], psTb[:])

        for n in range(8):
            p.op('tensor', 'matmul', psA[:, :], onesrb_s[0:1, :], invrow1[0:1, 512 * n:512 * (n + 1)],
                 start=True, stop=True)
            p.op('vector', 'tensor_copy', hf2[:, 512 * n:512 * (n + 1)], psA[:, :])

        p.op('vector', 'memset', hf3[:], 0.0)
        for i in range(32):
            p.dma(cw_t[:], convw[128 * i:128 * (i + 1), :], allow_slow_non_contiguous=True)
            p.dma(cb_t[:], convb[128 * i:128 * (i + 1), :], allow_slow_non_contiguous=True)
            for j in range(8):
                for dt in range(8):
                    p.op('tensor', 'matmul', psA[:, :],
                         big1[:, 4096 * dt + 128 * i: 4096 * dt + 128 * (i + 1)],
                         big2[:, 4096 * dt + 512 * j: 4096 * dt + 512 * (j + 1)],
                         start=(dt == 0), stop=(dt == 7))
                p.op('vector', 'tensor_scalar', hf5[:, 512 * j:512 * (j + 1)], psA[:, :],
                     scalar1=m1_col[:], scalar2=None, op0=ALU.mult)
                p.op('vector', 'tensor_tensor', hf1[:, 512 * j:512 * (j + 1)],
                     hf5[:, 512 * j:512 * (j + 1)], hf2[:, 512 * j:512 * (j + 1)], op=ALU.mult)
            for b in range(2):
                base = 2048 * b
                p.op('vector', 'tensor_scalar', hf5[:, 0:2048], hf1[:, base:base + 2048],
                     scalar1=cw_t[:, 1:2], scalar2=None, op0=ALU.mult)
                p.op('vector', 'tensor_scalar', hf5[:, 2048:4095], hf1[:, base:base + 2047],
                     scalar1=cw_t[:, 0:1], scalar2=None, op0=ALU.mult)
                p.op('vector', 'tensor_tensor', hf5[:, 1:2048], hf5[:, 1:2048],
                     hf5[:, 2048:4095], op=ALU.add)
                p.op('vector', 'tensor_scalar', hf5[:, 2048:4095], hf1[:, base + 1:base + 2048],
                     scalar1=cw_t[:, 2:3], scalar2=None, op0=ALU.mult)
                p.op('vector', 'tensor_tensor', hf5[:, 0:2047], hf5[:, 0:2047],
                     hf5[:, 2048:4095], op=ALU.add)
                p.op('scalar', 'activation', hf1[:, base:base + 2048], hf5[:, 0:2048], AF.Silu,
                     bias=cb_t[:])
            p.op('scalar', 'activation', hf5[:], hf1[:], AF.Abs)
            p.op('vector', 'tensor_tensor', hf3[:], hf3[:], hf5[:], op=ALU.max)
            p.op('vector', 'tensor_copy', qbf[:], hf1[:])
            p.dma(hsil[128 * i:128 * (i + 1), :], qbf[:])

        for j in range(32):
            p.op('tensor', 'transpose', psTb[:], hf3[:, 128 * j:128 * (j + 1)], identb_s[:])
            p.op('vector', 'reduce_max', c0[:], psTb[:], axis=AX.X)
            p.op('vector', 'tensor_scalar_max', c1[:], c0[:], 1e-5)
            p.op('vector', 'reciprocal', c2[:], c1[:])
            p.op('vector', 'tensor_scalar_mul', c2[:], c2[:], 127.0)
            p.op('tensor', 'transpose', psT[0:1, 0:128], c2[:], ident_s[:])
            p.op('scalar', 'copy', invrow1[0:1, 128 * j:128 * (j + 1)], psT[0:1, 0:128])
            p.op('vector', 'tensor_scalar', i2m2[:, j:j + 1], c1[:], scalar1=1.0 / 127,
                 scalar2=m2_col[:], op0=ALU.mult, op1=ALU.mult)

        for n in range(8):
            p.op('tensor', 'matmul', psA[:, :], onesrb_s[0:1, :], invrow1[0:1, 512 * n:512 * (n + 1)],
                 start=True, stop=True)
            p.op('vector', 'tensor_copy', hf3[:, 512 * n:512 * (n + 1)], psA[:, :])

        for i in range(32):
            p.dma(qbf[:], hsil[128 * i:128 * (i + 1), :])
            p.op('vector', 'tensor_tensor', hf5[:], qbf[:], hf3[:], op=ALU.mult)
            for cq in range(4):
                p.op('vector', 'tensor_copy', ti32[:], hf5[:, 1024 * cq:1024 * (cq + 1)])
                p.op('vector', 'tensor_scalar', qbf[:, 1024 * cq:1024 * (cq + 1)], ti32[:],
                     scalar1=-128.0, scalar2=127.0, op0=ALU.max, op1=ALU.min)
            p.dma(hq[128 * i:128 * (i + 1), :], qbf[:])

        for i in range(8):
            p.dma(hf1[:], w2[128 * i:128 * (i + 1), :])
            p.op('vector', 'tensor_scalar', hf5[:], hf1[:], scalar1=sc2w_col[:], scalar2=None,
                 op0=ALU.mult)
            for cq in range(4):
                p.op('vector', 'tensor_copy', ti32[:], hf5[:, 1024 * cq:1024 * (cq + 1)])
                p.op('vector', 'tensor_scalar', qbf[:, 1024 * cq:1024 * (cq + 1)], ti32[:],
                     scalar1=-1.0, scalar2=1.0, op0=ALU.max, op1=ALU.min)
            for ct in range(32):
                p.op('tensor', 'transpose', psTb[:], qbf[:, 128 * ct:128 * (ct + 1)], identb_s[:])
                p.op('scalar', 'copy', big2[:, 1024 * ct + 128 * i: 1024 * ct + 128 * (i + 1)], psTb[:])
        for i in range(32):
            p.dma(x_t[:], x[128 * i:128 * (i + 1), :])
            for n in range(2):
                for ct in range(32):
                    p.dma(hqt_s[:], hq[128 * ct:128 * (ct + 1), 128 * i:128 * (i + 1)])
                    p.op('tensor', 'matmul', psA[:, :], hqt_s[:],
                         big2[:, 1024 * ct + 512 * n: 1024 * ct + 512 * (n + 1)],
                         start=(ct == 0), stop=(ct == 31))
                p.op('vector', 'tensor_scalar', u_f[:, 512 * n:512 * (n + 1)], psA[:, :],
                     scalar1=i2m2[:, i:i + 1], scalar2=None, op0=ALU.mult)
            p.op('vector', 'tensor_tensor', u_f[:], u_f[:], x_t[:], op=ALU.add)
            p.dma(y[128 * i:128 * (i + 1), :], u_f[:])
        p.build(nc, g, blk)
    return nc


def l0_ffn_inmaps(inputs, x1full):
    import ml_dtypes
    common = {
        "w1": inputs["L0_ffn_w1"],
        "w2": inputs["L0_ffn_w2"],
        "convw": inputs["L0_ffn_conv_w"].reshape(4096, 3),
        "convb": inputs["L0_ffn_conv_b"].reshape(4096, 1),
        "identf": np.eye(128, dtype=np.float32),
        "identb": np.eye(128, dtype=np.float32).astype(ml_dtypes.bfloat16),
        "onesr": np.ones((1, 128), np.float32),
        "onesc": np.ones((128, 1), np.float32),
    }
    maps = []
    for c in range(NCORES):
        m = dict(common)
        m["x"] = np.ascontiguousarray(x1full[c * R:(c + 1) * R])
        maps.append(m)
    return maps



LCH = 256
NCH = T // LCH


def build_l1_attn():
    nc = bass.Bass("TRN2", target_bir_lowering=False)
    x = nc.dram_tensor("x", [R, D], F32, kind="ExternalInput").ap()
    maacols_d = nc.dram_tensor("maacols", [D, 6], F32, kind="ExternalInput").ap()
    maaw1 = nc.dram_tensor("maaw1", [D, 128], F32, kind="ExternalInput").ap()
    maaw2 = nc.dram_tensor("maaw2", [128, D], F32, kind="ExternalInput").ap()
    dw1 = nc.dram_tensor("dw1", [D, 64], F32, kind="ExternalInput").ap()
    dw2 = nc.dram_tensor("dw2", [64, D], F32, kind="ExternalInput").ap()
    fa = nc.dram_tensor("fa", [D, 1], F32, kind="ExternalInput").ap()
    wr = nc.dram_tensor("wr", [D, D], F32, kind="ExternalInput").ap()
    wk = nc.dram_tensor("wk", [D, D], F32, kind="ExternalInput").ap()
    wv = nc.dram_tensor("wv", [D, D], F32, kind="ExternalInput").ap()
    wo = nc.dram_tensor("wo", [D, D], F32, kind="ExternalInput").ap()
    lnw = nc.dram_tensor("lnw", [1, D], F32, kind="ExternalInput").ap()
    lnb = nc.dram_tensor("lnb", [1, D], F32, kind="ExternalInput").ap()
    identf = nc.dram_tensor("identf", [128, 128], F32, kind="ExternalInput").ap()
    identb = nc.dram_tensor("identb", [128, 128], BF16, kind="ExternalInput").ap()
    onesr = nc.dram_tensor("onesr", [1, 128], F32, kind="ExternalInput").ap()
    triu = nc.dram_tensor("triu", [128, 512], F32, kind="ExternalInput").ap()
    masku = nc.dram_tensor("masku", [128, 512], BF16, kind="ExternalInput").ap()
    ones2d = nc.dram_tensor("ones2", [128, 2], F32, kind="ExternalInput").ap()
    y = nc.dram_tensor("y", [R, D], F32, kind="ExternalOutput").ap()

    ctx = contextlib.ExitStack()
    with ctx:
        sb = lambda name, shape, dt: ctx.enter_context(nc.sbuf_tensor(name, shape, dt))
        wr_s = sb("wr_s", [128, 8192], BF16)
        wk_s = sb("wk_s", [128, 8192], BF16)
        wv_s = sb("wv_s", [128, 8192], BF16)
        wo_s = sb("wo_s", [128, 8192], BF16)
        maaw1_s = sb("maaw1_s", [128, 1024], BF16)
        maaw2f = [sb(f"maaw2f{f}", [32, 1024], BF16) for f in range(4)]
        dw1_s = sb("dw1_s", [128, 512], BF16)
        dw2_s = sb("dw2_s", [64, 1024], BF16)
        maacols = sb("maacols_s", [128, 48], F32)
        fa_cols = sb("fa_cols", [128, 8], F32)
        lnw_bc = sb("lnw_bc", [128, 1024], F32)
        lnb_bc = sb("lnb_bc", [128, 1024], F32)
        triu_s = sb("triu_s", [128, 512], F32)
        masku_s = sb("masku_s", [128, 512], BF16)
        ones2_s = sb("ones2_s", [128, 2], F32)
        ident_s = sb("ident_s", [128, 128], F32)
        identb_s = sb("identb_s", [128, 128], BF16)
        onesr_s = sb("onesr_s", [1, 128], F32)
        uTc = sb("uTc", [128, 2048], BF16)
        xxT = sb("xxT", [128, 2048], BF16)
        xxxT = sb("xxxT", [128, 2048], BF16)
        ximT = [sb(f"ximT{f}", [128, 2048], BF16) for f in range(4)]
        eT = sb("eT", [128, 2048], F32)
        etm2 = sb("etm2", [128, 256], F32)
        rT = sb("rT", [128, 2048], BF16)
        kT = sb("kT", [128, 2048], BF16)
        raT = sb("raT", [128, 2048], BF16)
        kbT = sb("kbT", [128, 2048], BF16)
        kbtm2 = sb("kbtm2", [128, 256], BF16)
        v_tmc = sb("v_tmc", [128, 2048], BF16)
        PTs = sb("PTs", [128, 512], BF16)
        ybig = sb("ybig", [128, 2048], F32)
        yn2 = sb("yn2", [128, 1024], BF16)
        ynT = sb("ynT", [128, 16384], BF16)
        Dcols = sb("Dcols", [128, 32], F32)
        d2 = sb("d2", [2, 256], F32)
        S_bf = sb("S_bf", [128, 512], BF16)
        gcols = sb("gcols", [128, 8], F32)
        th_f = [sb(f"th_f{f}", [32, 256], BF16) for f in range(4)]
        thd = sb("thd", [64, 256], BF16)
        aT = sb("aT", [128, 256], F32)
        bF = sb("bF", [128, 256], F32)
        tf1 = sb("tf1", [128, 256], F32)
        tf2 = sb("tf2", [128, 256], F32)
        tf64 = sb("tf64", [128, 64], F32)
        prevcol = sb("prevcol", [128, 8], BF16)
        colmat = sb("colmat", [128, 16], F32)
        sqmat = sb("sqmat", [128, 16], F32)
        gnsq = sb("gnsq", [128, 64], BF16)
        stat1 = sb("stat1", [128, 16], F32)
        stat2 = sb("stat2", [128, 16], F32)
        stat3 = sb("stat3", [128, 16], F32)
        epsg = sb("epsg", [128, 1], F32)
        x_t = sb("x_t", [128, 1024], F32)
        lns = ln_scratch(nc, ctx)
        psA = ctx.enter_context(nc.psum_tensor("psA", [128, 512], F32))
        psM = ctx.enter_context(nc.psum_tensor("psM", [64, 256], F32))
        psCum = ctx.enter_context(nc.psum_tensor("psCum", [128, 256], F32))
        psD = ctx.enter_context(nc.psum_tensor("psD", [2, 256], F32))
        psY = ctx.enter_context(nc.psum_tensor("psY", [128, 64], F32))
        psS = ctx.enter_context(nc.psum_tensor("psS", [64, 64], F32))
        psT = ctx.enter_context(nc.psum_tensor("psT", [128, 128], F32))
        psTb = ctx.enter_context(nc.psum_tensor("psTb", [128, 128], BF16))
        g = ctx.enter_context(nc.semaphore("g"))
        blk = ctx.enter_context(nc.Block())
        p = P()

        p.dma(ident_s[:], identf[:, :])
        p.dma(identb_s[:], identb[:, :])
        p.dma(onesr_s[:], onesr[:, :])
        p.dma(triu_s[:], triu[:, :])
        p.dma(masku_s[:], masku[:, :])
        p.dma(ones2_s[:], ones2d[:, :])
        for (w_ap, w_sb) in ((wr, wr_s), (wk, wk_s), (wv, wv_s), (wo, wo_s)):
            for i in range(8):
                p.dma(x_t[:], w_ap[128 * i:128 * (i + 1), :])
                p.op('vector', 'tensor_copy', w_sb[:, 1024 * i:1024 * (i + 1)], x_t[:])
        for i in range(8):
            p.dma(x_t[:, 0:128], maaw1[128 * i:128 * (i + 1), :])
            p.op('vector', 'tensor_copy', maaw1_s[:, 128 * i:128 * (i + 1)], x_t[:, 0:128])
        for f in range(4):
            p.dma(x_t[0:32, 0:1024], maaw2[32 * f:32 * (f + 1), :])
            p.op('vector', 'tensor_copy', maaw2f[f][:, :], x_t[0:32, 0:1024])
        for i in range(8):
            p.dma(x_t[:, 0:64], dw1[128 * i:128 * (i + 1), :])
            p.op('vector', 'tensor_copy', dw1_s[:, 64 * i:64 * (i + 1)], x_t[:, 0:64])
        p.dma(x_t[0:64, 0:1024], dw2[:, :])
        p.op('vector', 'tensor_copy', dw2_s[:, :], x_t[0:64, 0:1024])
        for j in range(6):
            for dt in range(8):
                p.dma(maacols[:, 8 * j + dt: 8 * j + dt + 1],
                      maacols_d[128 * dt:128 * (dt + 1), j:j + 1],
                      allow_slow_non_contiguous=True)
        for dt in range(8):
            p.dma(fa_cols[:, dt:dt + 1], fa[128 * dt:128 * (dt + 1), :],
                  allow_slow_non_contiguous=True)
        p.dma(x_t[0:1, 0:1024], lnw[:, :])
        for n in range(2):
            p.op('tensor', 'matmul', psA[:, :], onesr_s[0:1, :], x_t[0:1, 512 * n:512 * (n + 1)],
                 start=True, stop=True)
            p.op('vector', 'tensor_copy', lnw_bc[:, 512 * n:512 * (n + 1)], psA[:, :])
        p.dma(x_t[0:1, 0:1024], lnb[:, :])
        for n in range(2):
            p.op('tensor', 'matmul', psA[:, :], onesr_s[0:1, :], x_t[0:1, 512 * n:512 * (n + 1)],
                 start=True, stop=True)
            p.op('vector', 'tensor_copy', lnb_bc[:, 512 * n:512 * (n + 1)], psA[:, :])
        p.op('vector', 'memset', epsg[:], 0.00064)

        for b in range(BPC):
            p.op('vector', 'memset', S_bf[:], 0.0)
            for ci in range(NCH):
                g0 = 2048 * b + LCH * ci
                for tt in range(2):
                    p.dma(x_t[:], x[g0 + 128 * tt: g0 + 128 * (tt + 1), :])
                    emit_ln(p, x_t, yn2[:], lns)
                    for dt in range(8):
                        p.op('tensor', 'transpose', psTb[:], yn2[:, 128 * dt:128 * (dt + 1)],
                             identb_s[:])
                        p.op('scalar', 'copy', uTc[:, 256 * dt + 128 * tt: 256 * dt + 128 * (tt + 1)],
                             psTb[:])
                for dt in range(8):
                    c0_ = 256 * dt
                    if ci == 0:
                        p.op('vector', 'memset', xxT[:, c0_:c0_ + 1], 0.0)
                    else:
                        p.op('vector', 'tensor_copy', xxT[:, c0_:c0_ + 1], prevcol[:, dt:dt + 1])
                    p.op('vector', 'tensor_copy', xxT[:, c0_ + 1:c0_ + 256], uTc[:, c0_:c0_ + 255])
                    p.op('vector', 'tensor_copy', prevcol[:, dt:dt + 1], uTc[:, c0_ + 255:c0_ + 256])
                    p.op('vector', 'tensor_sub', xxT[:, c0_:c0_ + 256], xxT[:, c0_:c0_ + 256],
                         uTc[:, c0_:c0_ + 256])
                    p.op('vector', 'tensor_scalar', tf1[:], xxT[:, c0_:c0_ + 256],
                         scalar1=maacols[:, dt:dt + 1], scalar2=None, op0=ALU.mult)
                    p.op('vector', 'tensor_tensor', xxxT[:, c0_:c0_ + 256], tf1[:],
                         uTc[:, c0_:c0_ + 256], op=ALU.add)
                for f in range(4):
                    for dt in range(8):
                        p.op('tensor', 'matmul', psM[0:32, :],
                             maaw1_s[:, 128 * dt + 32 * f: 128 * dt + 32 * f + 32],
                             xxxT[:, 256 * dt:256 * (dt + 1)], start=(dt == 0), stop=(dt == 7))
                    p.op('scalar', 'activation', th_f[f][:, :], psM[0:32, :], AF.Tanh)
                for f in range(4):
                    for dt in range(8):
                        p.op('tensor', 'matmul', psA[:, 0:256], maaw2f[f][:, 128 * dt:128 * (dt + 1)],
                             th_f[f][:, :], start=True, stop=True)
                        p.op('vector', 'tensor_scalar', tf1[:], psA[:, 0:256],
                             scalar1=maacols[:, 8 * (f + 1) + dt: 8 * (f + 1) + dt + 1],
                             scalar2=None, op0=ALU.add)
                        p.op('vector', 'tensor_tensor', tf2[:], tf1[:], xxT[:, 256 * dt:256 * (dt + 1)],
                             op=ALU.mult)
                        p.op('vector', 'tensor_tensor', ximT[f][:, 256 * dt:256 * (dt + 1)], tf2[:],
                             uTc[:, 256 * dt:256 * (dt + 1)], op=ALU.add)
                for dt in range(8):
                    p.op('tensor', 'matmul', psM[:, :], dw1_s[:, 64 * dt:64 * (dt + 1)],
                         ximT[0][:, 256 * dt:256 * (dt + 1)], start=(dt == 0), stop=(dt == 7))
                p.op('scalar', 'activation', thd[:, :], psM[:, :], AF.Tanh)
                for dt in range(8):
                    p.op('tensor', 'matmul', psA[:, 0:256], dw2_s[:, 128 * dt:128 * (dt + 1)],
                         thd[:, :], start=True, stop=True)
                    p.op('scalar', 'activation', eT[:, 256 * dt:256 * (dt + 1)], psA[:, 0:256],
                         AF.Exp, bias=maacols[:, 40 + dt:40 + dt + 1])
                for (w_sb, dst) in ((wr_s, rT), (wk_s, kT)):
                    src = ximT[3] if dst is rT else ximT[1]
                    for mt in range(8):
                        for dt in range(8):
                            p.op('tensor', 'matmul', psA[:, 0:256],
                                 w_sb[:, 1024 * dt + 128 * mt: 1024 * dt + 128 * (mt + 1)],
                                 src[:, 256 * dt:256 * (dt + 1)], start=(dt == 0), stop=(dt == 7))
                        p.op('vector', 'tensor_copy', dst[:, 256 * mt:256 * (mt + 1)], psA[:, 0:256])
                for tt in range(2):
                    for n in range(2):
                        for dt in range(8):
                            p.op('tensor', 'matmul', psA[:, :],
                                 ximT[2][:, 256 * dt + 128 * tt: 256 * dt + 128 * (tt + 1)],
                                 wv_s[:, 1024 * dt + 512 * n: 1024 * dt + 512 * (n + 1)],
                                 start=(dt == 0), stop=(dt == 7))
                        p.op('vector', 'tensor_copy',
                             v_tmc[:, 1024 * tt + 512 * n: 1024 * tt + 512 * (n + 1)], psA[:, :])
                for dt in range(8):
                    c0_ = 256 * dt
                    for st in range(2):
                        p.op('tensor', 'transpose', psT[:],
                             eT[:, 256 * dt + 128 * st: 256 * dt + 128 * (st + 1)], ident_s[:])
                        p.op('scalar', 'copy', etm2[:, 128 * st:128 * (st + 1)], psT[:])
                    for st in range(2):
                        p.op('tensor', 'matmul', psCum[:, :],
                             etm2[:, 128 * st:128 * (st + 1)],
                             triu_s[:, 256 * st:256 * (st + 1)], start=(st == 0), stop=(st == 1))
                    p.op('vector', 'tensor_tensor', tf1[:], eT[:, c0_:c0_ + 256], psCum[:, :],
                         op=ALU.subtract)
                    p.op('scalar', 'activation', aT[:], tf1[:], AF.Exp)
                    p.op('vector', 'tensor_tensor', raT[:, c0_:c0_ + 256], rT[:, c0_:c0_ + 256],
                         aT[:], op=ALU.mult)
                    p.op('scalar', 'activation', bF[:], psCum[:, :], AF.Exp)
                    p.op('vector', 'tensor_tensor', kbT[:, c0_:c0_ + 256], kT[:, c0_:c0_ + 256],
                         bF[:], op=ALU.mult)
                    p.op('scalar', 'activation', gcols[:, dt:dt + 1], psCum[:, 255:256], AF.Exp,
                         scale=-1.0)
                for dt in range(8):
                    p.op('vector', 'tensor_tensor', tf1[:], rT[:, 256 * dt:256 * (dt + 1)],
                         kT[:, 256 * dt:256 * (dt + 1)], op=ALU.mult)
                    p.op('vector', 'tensor_scalar', tf2[:], tf1[:], scalar1=fa_cols[:, dt:dt + 1],
                         scalar2=None, op0=ALU.mult)
                    p.op('tensor', 'matmul', psD[:, :], ones2_s[:, :], tf2[:], start=True, stop=True)
                    p.op('vector', 'tensor_copy', d2[:, :], psD[:, :])
                    for st in range(2):
                        p.op('tensor', 'transpose', psT[0:128, 0:2],
                             d2[0:2, 128 * st:128 * (st + 1)], ident_s[0:2, 0:2])
                        p.op('vector', 'tensor_copy',
                             Dcols[:, 16 * st + 2 * dt: 16 * st + 2 * dt + 2], psT[0:128, 0:2])
                for h in range(16):
                    dt = h // 2
                    po = 64 * (h % 2)
                    if h % 2 == 0:
                        for st in range(2):
                            p.op('tensor', 'transpose', psTb[:],
                                 kbT[:, 256 * dt + 128 * st: 256 * dt + 128 * (st + 1)], identb_s[:])
                            p.op('scalar', 'copy', kbtm2[:, 128 * st:128 * (st + 1)], psTb[:])
                    for st in range(2):
                        p.op('tensor', 'matmul', psA[:, 0:256],
                             kbT[po:po + 64, 256 * dt + 128 * st: 256 * dt + 128 * (st + 1)],
                             raT[po:po + 64, 256 * dt:256 * (dt + 1)], start=True, stop=True)
                        p.op('vector', 'tensor_tensor', PTs[:, 256 * st:256 * (st + 1)],
                             psA[:, 0:256], masku_s[:, 256 * st:256 * (st + 1)], op=ALU.mult)
                    for tt in range(2):
                        for st in range(2):
                            p.op('tensor', 'matmul', psY[:, :],
                                 PTs[:, 256 * st + 128 * tt: 256 * st + 128 * (tt + 1)],
                                 v_tmc[:, 1024 * st + 64 * h: 1024 * st + 64 * h + 64],
                                 start=(st == 0), stop=False)
                        p.op('tensor', 'matmul', psY[:, :],
                             raT[po:po + 64, 256 * dt + 128 * tt: 256 * dt + 128 * (tt + 1)],
                             S_bf[po:po + 64, 64 * dt:64 * (dt + 1)], start=False, stop=True)
                        p.op('vector', 'tensor_scalar', tf64[:],
                             v_tmc[:, 1024 * tt + 64 * h: 1024 * tt + 64 * h + 64],
                             scalar1=Dcols[:, 16 * tt + 2 * dt + (h % 2):
                                           16 * tt + 2 * dt + (h % 2) + 1],
                             scalar2=None, op0=ALU.mult)
                        p.op('vector', 'tensor_tensor',
                             ybig[:, 1024 * tt + 64 * h: 1024 * tt + 64 * h + 64], tf64[:],
                             psY[:, :], op=ALU.add)
                    for st in range(2):
                        p.op('tensor', 'matmul', psS[:, :],
                             kbtm2[:, 128 * st + 64 * (h % 2): 128 * st + 64 * (h % 2) + 64],
                             v_tmc[:, 1024 * st + 64 * h: 1024 * st + 64 * h + 64],
                             start=(st == 0), stop=(st == 1))
                    p.op('vector', 'tensor_tensor', tf64[0:64, :],
                         S_bf[po:po + 64, 64 * dt:64 * (dt + 1)], psS[:, :], op=ALU.add)
                    p.op('vector', 'tensor_scalar', S_bf[po:po + 64, 64 * dt:64 * (dt + 1)],
                         tf64[0:64, :], scalar1=gcols[po:po + 64, dt:dt + 1], scalar2=None,
                         op0=ALU.mult)
                for tt in range(2):
                    for h in range(16):
                        p.op('vector', 'reduce_sum', colmat[:, h:h + 1],
                             ybig[:, 1024 * tt + 64 * h: 1024 * tt + 64 * h + 64], axis=AX.X)
                        p.op('scalar', 'activation', gnsq[:, :],
                             ybig[:, 1024 * tt + 64 * h: 1024 * tt + 64 * h + 64], AF.Square,
                             accum_out=sqmat[:, h:h + 1])
                    p.op('vector', 'tensor_scalar_mul', stat1[:], colmat[:], 1.0 / 64)
                    p.op('vector', 'tensor_scalar_mul', stat2[:], sqmat[:], 1.0 / 64)
                    p.op('vector', 'tensor_mul', stat3[:], stat1[:], stat1[:])
                    p.op('vector', 'tensor_sub', stat2[:], stat2[:], stat3[:])
                    p.op('scalar', 'activation', stat2[:], stat2[:], AF.Sqrt, bias=epsg[:])
                    p.op('vector', 'reciprocal', stat2[:], stat2[:])
                    for h in range(16):
                        p.op('vector', 'tensor_scalar', yn2[:, 64 * h:64 * (h + 1)],
                             ybig[:, 1024 * tt + 64 * h: 1024 * tt + 64 * h + 64],
                             scalar1=stat1[:, h:h + 1], scalar2=stat2[:, h:h + 1],
                             op0=ALU.subtract, op1=ALU.mult)
                    p.op('vector', 'tensor_tensor', yn2[:], yn2[:], lnw_bc[:], op=ALU.mult)
                    p.op('vector', 'tensor_tensor', yn2[:], yn2[:], lnb_bc[:], op=ALU.add)
                    for dt in range(8):
                        p.op('tensor', 'transpose', psTb[:], yn2[:, 128 * dt:128 * (dt + 1)],
                             identb_s[:])
                        p.op('scalar', 'copy',
                             ynT[:, 2048 * dt + 256 * ci + 128 * tt:
                                 2048 * dt + 256 * ci + 128 * (tt + 1)], psTb[:])
            for i in range(16):
                g0 = 2048 * b + 128 * i
                p.dma(x_t[:], x[g0:g0 + 128, :])
                for n in range(2):
                    for dt in range(8):
                        p.op('tensor', 'matmul', psA[:, :],
                             ynT[:, 2048 * dt + 128 * i: 2048 * dt + 128 * (i + 1)],
                             wo_s[:, 1024 * dt + 512 * n: 1024 * dt + 512 * (n + 1)],
                             start=(dt == 0), stop=(dt == 7))
                    p.op('vector', 'tensor_tensor', eT[:, 512 * n:512 * (n + 1)], psA[:, :],
                         x_t[:, 512 * n:512 * (n + 1)], op=ALU.add)
                p.dma(y[g0:g0 + 128, :], eT[:, 0:1024])
        p.build(nc, g, blk)
    return nc


def l1_attn_inmaps(inputs, x2full):
    import ml_dtypes
    maacols = np.stack([
        inputs["L1_attn_time_maa_x"], inputs["L1_attn_time_maa_w"],
        inputs["L1_attn_time_maa_k"], inputs["L1_attn_time_maa_v"],
        inputs["L1_attn_time_maa_r"], inputs["L1_attn_time_decay"]], axis=1).astype(np.float32)
    triu = np.zeros((128, 512), np.float32)
    masku = np.zeros((128, 512), np.float32)
    for st in range(2):
        for pp in range(128):
            s_ = 128 * st + pp
            t_ = np.arange(256)
            triu[pp, 256 * st:256 * (st + 1)] = (s_ <= t_).astype(np.float32)
            masku[pp, 256 * st:256 * (st + 1)] = (s_ < t_).astype(np.float32)
    common = {
        "maacols": maacols,
        "maaw1": inputs["L1_attn_time_maa_w1"],
        "maaw2": inputs["L1_attn_time_maa_w2"].reshape(128, 1024),
        "dw1": inputs["L1_attn_time_decay_w1"],
        "dw2": inputs["L1_attn_time_decay_w2"],
        "fa": inputs["L1_attn_time_faaaa"].reshape(1024, 1),
        "wr": inputs["L1_attn_W_r"], "wk": inputs["L1_attn_W_k"],
        "wv": inputs["L1_attn_W_v"], "wo": inputs["L1_attn_W_o"],
        "lnw": inputs["L1_attn_ln_w"].reshape(1, -1),
        "lnb": inputs["L1_attn_ln_b"].reshape(1, -1),
        "identf": np.eye(128, dtype=np.float32),
        "identb": np.eye(128, dtype=np.float32).astype(ml_dtypes.bfloat16),
        "onesr": np.ones((1, 128), np.float32),
        "triu": triu,
        "masku": masku.astype(ml_dtypes.bfloat16),
        "ones2": np.concatenate([np.concatenate([np.ones((64, 1)), np.zeros((64, 1))]),
                                 np.concatenate([np.zeros((64, 1)), np.ones((64, 1))])],
                                axis=1).astype(np.float32),
    }
    maps = []
    for c in range(NCORES):
        m = dict(common)
        m["x"] = np.ascontiguousarray(x2full[c * R:(c + 1) * R])
        maps.append(m)
    return maps



def build_l1_route():
    nc = bass.Bass("TRN2", target_bir_lowering=False)
    x = nc.dram_tensor("x", [R, D], F32, kind="ExternalInput").ap()
    rw = nc.dram_tensor("rw", [64, D], F32, kind="ExternalInput").ap()
    ebias = nc.dram_tensor("ebias", [1, 64], F32, kind="ExternalInput").ap()
    identf = nc.dram_tensor("identf", [128, 128], F32, kind="ExternalInput").ap()
    identb = nc.dram_tensor("identb", [128, 128], BF16, kind="ExternalInput").ap()
    onesr = nc.dram_tensor("onesr", [1, 128], F32, kind="ExternalInput").ap()
    asgT = nc.dram_tensor("asgT", [64, R], F32, kind="ExternalOutput").ap()

    ctx = contextlib.ExitStack()
    with ctx:
        sb = lambda name, shape, dt: ctx.enter_context(nc.sbuf_tensor(name, shape, dt))
        ident_s = sb("ident_s", [128, 128], F32)
        identb_s = sb("identb_s", [128, 128], BF16)
        onesr_s = sb("onesr_s", [1, 128], F32)
        rwT_s = sb("rwT_s", [128, 512], BF16)
        ebias_bc = sb("ebias_bc", [128, 64], F32)
        x_t = sb("x_t", [128, 1024], F32)
        u_bf = sb("u_bf", [128, 1024], BF16)
        uT_t = sb("uT_t", [128, 1024], BF16)
        scores = sb("scores", [128, 64], F32)
        self_ = sb("self_", [128, 64], F32)
        asg = sb("asg", [128, 64], F32)
        eqm = sb("eqm", [128, 64], F32)
        asg64 = sb("asg64", [64, 128], F32)
        mx = sb("mx", [128, 1], F32)
        lns = ln_scratch(nc, ctx)
        psA = ctx.enter_context(nc.psum_tensor("psA", [128, 512], F32))
        psE2 = ctx.enter_context(nc.psum_tensor("psE2", [128, 64], F32))
        psT = ctx.enter_context(nc.psum_tensor("psT", [128, 128], F32))
        psTb = ctx.enter_context(nc.psum_tensor("psTb", [128, 128], BF16))
        g = ctx.enter_context(nc.semaphore("g"))
        blk = ctx.enter_context(nc.Block())
        p = P()

        p.dma(ident_s[:], identf[:, :])
        p.dma(identb_s[:], identb[:, :])
        p.dma(onesr_s[:], onesr[:, :])
        p.dma(x_t[0:64, :], rw[:, :])
        p.op('vector', 'tensor_copy', u_bf[0:64, :], x_t[0:64, :])
        for dt in range(8):
            p.op('tensor', 'transpose', psTb[:, 0:64], u_bf[0:64, 128 * dt:128 * (dt + 1)],
                 identb_s[0:64, 0:64])
            p.op('scalar', 'copy', rwT_s[:, 64 * dt:64 * (dt + 1)], psTb[:, 0:64])
        p.dma(x_t[0:1, 0:64], ebias[:, :])
        p.op('tensor', 'matmul', psE2[:, :], onesr_s[0:1, :], x_t[0:1, 0:64], start=True, stop=True)
        p.op('vector', 'tensor_copy', ebias_bc[:], psE2[:, :])

        for i in range(NT):
            p.dma(x_t[:], x[128 * i:128 * (i + 1), :])
            emit_ln(p, x_t, u_bf[:], lns)
            for dt in range(8):
                p.op('tensor', 'transpose', psTb[:], u_bf[:, 128 * dt:128 * (dt + 1)], identb_s[:])
                p.op('scalar', 'copy', uT_t[:, 128 * dt:128 * (dt + 1)], psTb[:])
            for dt in range(8):
                p.op('tensor', 'matmul', psE2[:, :], uT_t[:, 128 * dt:128 * (dt + 1)],
                     rwT_s[:, 64 * dt:64 * (dt + 1)], start=(dt == 0), stop=(dt == 7))
            p.op('scalar', 'activation', scores[:], psE2[:, :], AF.Sigmoid)
            p.op('vector', 'tensor_tensor', self_[:], scores[:], ebias_bc[:], op=ALU.add)
            p.op('vector', 'memset', asg[:], 0.0)
            for k in range(6):
                p.op('vector', 'reduce_max', mx[:], self_[:], axis=AX.X)
                p.op('vector', 'tensor_scalar', eqm[:], self_[:], scalar1=mx[:], scalar2=None,
                     op0=ALU.is_equal)
                p.op('vector', 'tensor_tensor', asg[:], asg[:], eqm[:], op=ALU.add)
                p.op('vector', 'tensor_scalar', eqm[:], eqm[:], scalar1=1e9, scalar2=None,
                     op0=ALU.mult)
                p.op('vector', 'tensor_sub', self_[:], self_[:], eqm[:])
            p.op('tensor', 'transpose', psT[0:64, 0:128], asg[:], ident_s[:])
            p.op('vector', 'tensor_copy', asg64[:, :], psT[0:64, 0:128])
            p.dma(asgT[0:64, 128 * i:128 * (i + 1)], asg64[:, :])
        p.build(nc, g, blk)
    return nc


def l1_ffn_inmaps_route(inputs, x3full):
    import ml_dtypes
    common = {
        "rw": inputs["L1_ffn_router_weight"],
        "ebias": inputs["L1_ffn_expert_bias"].reshape(1, -1),
        "identf": np.eye(128, dtype=np.float32),
        "identb": np.eye(128, dtype=np.float32).astype(ml_dtypes.bfloat16),
        "onesr": np.ones((1, 128), np.float32),
    }
    maps = []
    for c in range(NCORES):
        m = dict(common)
        m["x"] = np.ascontiguousarray(x3full[c * R:(c + 1) * R])
        maps.append(m)
    return maps


NSLOT = 64 * 384


def emit_gelu_tanh(p, ps_ap, out_ap, s1, s2, hw):
    p.op('scalar', 'copy', s2, ps_ap)
    p.op('vector', 'tensor_tensor', s1, s2, s2, op=ALU.mult)
    p.op('vector', 'tensor_tensor', s1, s1, s2, op=ALU.mult)
    p.op('vector', 'tensor_scalar', s1, s1, scalar1=0.044715, scalar2=None, op0=ALU.mult)
    p.op('vector', 'tensor_tensor', s1, s1, s2, op=ALU.add)
    p.op('scalar', 'activation', s1, s1, AF.Tanh, scale=0.7978845608028654)
    p.op('vector', 'tensor_scalar', s1, s1, scalar1=1.0, scalar2=0.5, op0=ALU.add, op1=ALU.mult)
    p.op('vector', 'tensor_tensor', out_ap, s1, s2, op=ALU.mult)


def build_l1_prep():
    nc = bass.Bass("TRN2", target_bir_lowering=False)
    x = nc.dram_tensor("x", [R, D], F32, kind="ExternalInput").ap()
    asgF = nc.dram_tensor("asgF", [64, 28672], F32, kind="ExternalInput").ap()
    rw = nc.dram_tensor("rw", [64, D], F32, kind="ExternalInput").ap()
    ebias = nc.dram_tensor("ebias", [1, 64], F32, kind="ExternalInput").ap()
    sw1 = nc.dram_tensor("sw1", [2048, D], F32, kind="ExternalInput").ap()
    sw2 = nc.dram_tensor("sw2", [D, 2048], F32, kind="ExternalInput").ap()
    identf = nc.dram_tensor("identf", [128, 128], F32, kind="ExternalInput").ap()
    identb = nc.dram_tensor("identb", [128, 128], BF16, kind="ExternalInput").ap()
    onesr = nc.dram_tensor("onesr", [1, 128], F32, kind="ExternalInput").ap()
    onesc = nc.dram_tensor("onesc", [128, 1], F32, kind="ExternalInput").ap()
    iotac = nc.dram_tensor("iotac", [128, 1], F32, kind="ExternalInput").ap()
    iotee = nc.dram_tensor("iotee", [128, 64], F32, kind="ExternalInput").ap()
    strup = nc.dram_tensor("strup", [128, 128], F32, kind="ExternalInput").ap()
    ysh = nc.dram_tensor("ysh", [R, D], F32, kind="ExternalOutput").ap()
    ubo = nc.dram_tensor("ubo", [R, D], BF16, kind="ExternalOutput").ap()
    slotso = nc.dram_tensor("slotso", [128, 192], U32, kind="ExternalOutput").ap()
    wso = nc.dram_tensor("wso", [128, 192], F32, kind="ExternalOutput").ap()

    ctx = contextlib.ExitStack()
    with ctx:
        sb = lambda name, shape, dt: ctx.enter_context(nc.sbuf_tensor(name, shape, dt))
        sw1T_s = sb("sw1T_s", [128, 16384], BF16)
        sw2T_s = sb("sw2T_s", [128, 16384], BF16)
        uTall = sb("uTall", [128, 8 * R], BF16)
        gT_c = sb("gT_c", [128, 16 * 512], BF16)
        rwT_s = sb("rwT_s", [128, 512], BF16)
        ebias_bc = sb("ebias_bc", [128, 64], F32)
        iotee_s = sb("iotee_s", [128, 64], F32)
        strup_s = sb("strup_s", [128, 128], F32)
        bsbc = sb("bsbc", [128, 64], F32)
        basec = sb("basec", [64, 1], F32)
        base_r = sb("base_r", [1, 64], F32)
        carry_r = sb("carry_r", [1, 64], F32)
        asg_sl = sb("asg_sl", [64, 2048], F32)
        ident_s = sb("ident_s", [128, 128], F32)
        identb_s = sb("identb_s", [128, 128], BF16)
        onesr_s = sb("onesr_s", [1, 128], F32)
        onesc_s = sb("onesc_s", [128, 1], F32)
        iotac_s = sb("iotac_s", [128, 1], F32)
        x_t = sb("x_t", [128, 1024], F32)
        u_bf = sb("u_bf", [128, 1024], BF16)
        scores = sb("scores", [128, 64], F32)
        self_ = sb("self_", [128, 64], F32)
        asg = sb("asg", [128, 64], F32)
        eqm = sb("eqm", [128, 64], F32)
        mks = sb("mks", [128, 384], F32)
        cumsb = sb("cumsb", [128, 64], F32)
        wraw = sb("wraw", [128, 6], F32)
        tsum = sb("tsum", [128, 1], F32)
        mx = sb("mx", [128, 1], F32)
        c0 = sb("c0", [128, 1], F32)
        c1 = sb("c1", [128, 1], F32)
        c2 = sb("c2", [128, 1], F32)
        c3 = sb("c3", [128, 1], F32)
        slotstore = sb("slotstore", [128, 192], U32)
        wstore = sb("wstore", [128, 192], F32)
        slotf = sb("slotf", [128, 1], F32)
        gacc = sb("gacc", [128, 1024], F32)
        scr1 = sb("scr1", [128, 512], F32)
        scr2 = sb("scr2", [128, 512], F32)
        lns = ln_scratch(nc, ctx)
        psA = ctx.enter_context(nc.psum_tensor("psA", [128, 512], F32))
        psCum = ctx.enter_context(nc.psum_tensor("psCum", [128, 64], F32))
        psE2 = ctx.enter_context(nc.psum_tensor("psE2", [128, 64], F32))
        psEc = ctx.enter_context(nc.psum_tensor("psEc", [1, 64], F32))
        psT = ctx.enter_context(nc.psum_tensor("psT", [128, 128], F32))
        psTb = ctx.enter_context(nc.psum_tensor("psTb", [128, 128], BF16))
        g = ctx.enter_context(nc.semaphore("g"))
        blk = ctx.enter_context(nc.Block())
        p = P()

        p.dma(ident_s[:], identf[:, :])
        p.dma(identb_s[:], identb[:, :])
        p.dma(onesr_s[:], onesr[:, :])
        p.dma(onesc_s[:], onesc[:, :])
        p.dma(iotac_s[:], iotac[:, :])
        p.dma(iotee_s[:], iotee[:, :])
        p.dma(strup_s[:], strup[:, :])
        p.dma(x_t[0:64, :], rw[:, :])
        p.op('vector', 'tensor_copy', u_bf[0:64, :], x_t[0:64, :])
        for dt in range(8):
            p.op('tensor', 'transpose', psTb[:, 0:64], u_bf[0:64, 128 * dt:128 * (dt + 1)],
                 identb_s[0:64, 0:64])
            p.op('scalar', 'copy', rwT_s[:, 64 * dt:64 * (dt + 1)], psTb[:, 0:64])
        p.dma(x_t[0:1, 0:64], ebias[:, :])
        p.op('tensor', 'matmul', psE2[:, :], onesr_s[0:1, :], x_t[0:1, 0:64], start=True, stop=True)
        p.op('vector', 'tensor_copy', ebias_bc[:], psE2[:, :])
        for i in range(16):
            p.dma(x_t[:], sw1[128 * i:128 * (i + 1), :])
            p.op('vector', 'tensor_copy', u_bf[:], x_t[:])
            for dt in range(8):
                p.op('tensor', 'transpose', psTb[:], u_bf[:, 128 * dt:128 * (dt + 1)], identb_s[:])
                p.op('scalar', 'copy', sw1T_s[:, 2048 * dt + 128 * i: 2048 * dt + 128 * (i + 1)],
                     psTb[:])
        for i in range(8):
            p.dma(gacc[:, 0:1024], sw2[128 * i:128 * (i + 1), 0:1024])
            p.dma(scr1[:], sw2[128 * i:128 * (i + 1), 1024:1536])
            p.dma(scr2[:], sw2[128 * i:128 * (i + 1), 1536:2048])
            p.op('vector', 'tensor_copy', u_bf[:], gacc[:, 0:1024])
            for ft in range(8):
                p.op('tensor', 'transpose', psTb[:], u_bf[:, 128 * ft:128 * (ft + 1)], identb_s[:])
                p.op('scalar', 'copy', sw2T_s[:, 1024 * ft + 128 * i: 1024 * ft + 128 * (i + 1)],
                     psTb[:])
            p.op('vector', 'tensor_copy', u_bf[:, 0:512], scr1[:])
            p.op('vector', 'tensor_copy', u_bf[:, 512:1024], scr2[:])
            for ft in range(8):
                p.op('tensor', 'transpose', psTb[:], u_bf[:, 128 * ft:128 * (ft + 1)], identb_s[:])
                p.op('scalar', 'copy',
                     sw2T_s[:, 1024 * (8 + ft) + 128 * i: 1024 * (8 + ft) + 128 * (i + 1)], psTb[:])
        p.op('vector', 'memset', basec[:], 0.0)
        for j in range(14):
            p.dma(asg_sl[:], asgF[0:64, 2048 * j:2048 * (j + 1)])
            p.op('vector', 'reduce_sum', c0[0:64, :], asg_sl[:], axis=AX.X)
            p.op('vector', 'tensor_tensor', basec[:], basec[:], c0[0:64, :], op=ALU.add)
        p.op('tensor', 'transpose', psT[0:1, 0:64], basec[:], ident_s[0:64, 0:64])
        p.op('vector', 'tensor_copy', base_r[:, :], psT[0:1, 0:64])
        p.op('tensor', 'matmul', psE2[:, :], onesr_s[0:1, :], base_r[0:1, :], start=True, stop=True)
        p.op('vector', 'tensor_copy', bsbc[:], psE2[:, :])
        p.op('vector', 'memset', carry_r[:], 0.0)

        for i in range(NT):
            p.dma(x_t[:], x[128 * i:128 * (i + 1), :])
            emit_ln(p, x_t, u_bf[:], lns)
            p.dma(ubo[128 * i:128 * (i + 1), :], u_bf[:])
            for dt in range(8):
                p.op('tensor', 'transpose', psTb[:], u_bf[:, 128 * dt:128 * (dt + 1)], identb_s[:])
                p.op('scalar', 'copy', uTall[:, R * dt + 128 * i: R * dt + 128 * (i + 1)], psTb[:])
            for dt in range(8):
                p.op('tensor', 'matmul', psE2[:, :], uTall[:, R * dt + 128 * i: R * dt + 128 * (i + 1)],
                     rwT_s[:, 64 * dt:64 * (dt + 1)], start=(dt == 0), stop=(dt == 7))
            p.op('scalar', 'activation', scores[:], psE2[:, :], AF.Sigmoid)
            p.op('vector', 'tensor_tensor', self_[:], scores[:], ebias_bc[:], op=ALU.add)
            p.op('vector', 'memset', asg[:], 0.0)
            p.op('vector', 'memset', tsum[:], 0.0)
            for k in range(6):
                p.op('vector', 'reduce_max', mx[:], self_[:], axis=AX.X)
                p.op('vector', 'tensor_scalar', eqm[:], self_[:], scalar1=mx[:], scalar2=None,
                     op0=ALU.is_equal)
                p.op('vector', 'tensor_copy', mks[:, 64 * k:64 * (k + 1)], eqm[:])
                p.op('vector', 'tensor_tensor', asg[:], asg[:], eqm[:], op=ALU.add)
                p.op('vector', 'tensor_tensor', eqm[:], eqm[:], scores[:], op=ALU.mult)
                p.op('vector', 'reduce_sum', wraw[:, k:k + 1], eqm[:], axis=AX.X)
                p.op('vector', 'tensor_tensor', tsum[:], tsum[:], wraw[:, k:k + 1], op=ALU.add)
                p.op('vector', 'tensor_scalar', eqm[:], mks[:, 64 * k:64 * (k + 1)], scalar1=1e9,
                     scalar2=None, op0=ALU.mult)
                p.op('vector', 'tensor_sub', self_[:], self_[:], eqm[:])
            p.op('vector', 'tensor_scalar', tsum[:], tsum[:], scalar1=1e-20, scalar2=None, op0=ALU.add)
            p.op('vector', 'reciprocal', tsum[:], tsum[:])
            p.op('vector', 'tensor_scalar', wstore[:, 6 * i:6 * (i + 1)], wraw[:, 0:6],
                 scalar1=tsum[:], scalar2=None, op0=ALU.mult)
            p.op('tensor', 'matmul', psCum[:, :], strup_s[:, :], asg[:], start=True, stop=False)
            p.op('tensor', 'matmul', psCum[:, :], onesr_s[0:1, :], carry_r[0:1, :],
                 start=False, stop=True)
            p.op('vector', 'tensor_copy', cumsb[:], psCum[:, :])
            p.op('tensor', 'matmul', psEc[:, :], onesc_s[:, :], asg[:], start=True, stop=True)
            p.op('vector', 'tensor_tensor', carry_r[0:1, :], carry_r[0:1, :], psEc[0:1, :], op=ALU.add)
            for k in range(6):
                mk = mks[:, 64 * k:64 * (k + 1)]
                p.op('vector', 'tensor_tensor', eqm[:], mk, iotee_s[:], op=ALU.mult)
                p.op('vector', 'reduce_sum', c0[:], eqm[:], axis=AX.X)
                p.op('vector', 'tensor_tensor', eqm[:], mk, cumsb[:], op=ALU.mult)
                p.op('vector', 'reduce_sum', c1[:], eqm[:], axis=AX.X)
                p.op('vector', 'tensor_tensor', eqm[:], mk, bsbc[:], op=ALU.mult)
                p.op('vector', 'reduce_sum', c2[:], eqm[:], axis=AX.X)
                p.op('vector', 'tensor_tensor', c2[:], c2[:], c1[:], op=ALU.add)
                p.op('vector', 'tensor_scalar', c3[:], c2[:], scalar1=384.0, scalar2=None,
                     op0=ALU.is_lt)
                p.op('vector', 'tensor_scalar', slotf[:], c0[:], scalar1=384.0, scalar2=None,
                     op0=ALU.mult)
                p.op('vector', 'tensor_tensor', slotf[:], slotf[:], c1[:], op=ALU.add)
                p.op('vector', 'tensor_scalar', c3[:], c3[:], scalar1=-1.0, scalar2=1.0,
                     op0=ALU.mult, op1=ALU.add)
                p.op('vector', 'tensor_scalar', c2[:], c3[:], scalar1=1e8, scalar2=None, op0=ALU.mult)
                p.op('vector', 'tensor_tensor', slotf[:], slotf[:], c2[:], op=ALU.add)
                p.op('vector', 'tensor_tensor', c2[:], c3[:], iotac_s[:], op=ALU.mult)
                p.op('vector', 'tensor_tensor', slotf[:], slotf[:], c2[:], op=ALU.add)
                p.op('vector', 'tensor_copy', slotstore[:, 6 * i + k:6 * i + k + 1], slotf[:])
        p.dma(slotso[:, :], slotstore[:])
        p.dma(wso[:, :], wstore[:])

        for tc in range(8):
            for ft in range(16):
                for dt in range(8):
                    p.op('tensor', 'matmul', psA[:, :],
                         sw1T_s[:, 2048 * dt + 128 * ft: 2048 * dt + 128 * (ft + 1)],
                         uTall[:, R * dt + 512 * tc: R * dt + 512 * (tc + 1)],
                         start=(dt == 0), stop=(dt == 7))
                emit_gelu_tanh(p, psA[:, :], gT_c[:, 512 * ft:512 * (ft + 1)], scr1[:], scr2[:], 512)
            for tt in range(4):
                g0 = 512 * tc + 128 * tt
                p.dma(x_t[:], x[g0:g0 + 128, :])
                for n in range(2):
                    for ft in range(16):
                        p.op('tensor', 'matmul', psA[:, :],
                             gT_c[:, 512 * ft + 128 * tt: 512 * ft + 128 * (tt + 1)],
                             sw2T_s[:, 1024 * ft + 512 * n: 1024 * ft + 512 * (n + 1)],
                             start=(ft == 0), stop=(ft == 15))
                    p.op('vector', 'tensor_tensor', gacc[:, 512 * n:512 * (n + 1)], psA[:, :],
                         x_t[:, 512 * n:512 * (n + 1)], op=ALU.add)
                p.dma(ysh[g0:g0 + 128, :], gacc[:, 0:1024])
        p.build(nc, g, blk)
    return nc


def build_l1_scat(tiles):
    nc = bass.Bass("TRN2", target_bir_lowering=False)
    ub = nc.dram_tensor("ub", [R, D], BF16, kind="ExternalInput").ap()
    slotsi = nc.dram_tensor("slotsi", [128, 192], U32, kind="ExternalInput").ap()
    XS = nc.dram_tensor("XS", [NSLOT + 1, D], BF16, kind="ExternalOutput").ap()
    ctx = contextlib.ExitStack()
    with ctx:
        sb = lambda name, shape, dt: ctx.enter_context(nc.sbuf_tensor(name, shape, dt))
        u_bf = sb("u_bf", [128, 1024], BF16)
        slots = sb("slots", [128, 192], U32)
        g = ctx.enter_context(nc.semaphore("g"))
        blk = ctx.enter_context(nc.Block())
        p = P()
        p.dma(slots[:], slotsi[:, :])
        for i in tiles:
            p.dma(u_bf[:], ub[128 * i:128 * (i + 1), :])
            for k in range(6):
                p.st('gpsimd', lambda e, col=slots[:, 6 * i + k:6 * i + k + 1]: e.indirect_dma_start(
                    out=XS[:, :], out_offset=IOA(col, 0), in_=u_bf[:, :], in_offset=None,
                    bounds_check=NSLOT - 1, oob_is_err=False), 16)
        p.build(nc, g, blk)
    return nc


def build_l1_exp(npart):
    nc = bass.Bass("TRN2", target_bir_lowering=False)
    XSp = [nc.dram_tensor(f"XS{j}", [NSLOT + 1, D], BF16, kind="ExternalInput").ap()
           for j in range(npart)]
    w1e = nc.dram_tensor("w1e", [64 * 1024, 512], F32, kind="ExternalInput").ap()
    w2e = nc.dram_tensor("w2e", [64 * 512, 1024], F32, kind="ExternalInput").ap()
    identb = nc.dram_tensor("identb", [128, 128], BF16, kind="ExternalInput").ap()
    YS = nc.dram_tensor("YS", [NSLOT + 1, D], BF16, kind="ExternalOutput").ap()
    ctx = contextlib.ExitStack()
    with ctx:
        sb = lambda name, shape, dt: ctx.enter_context(nc.sbuf_tensor(name, shape, dt))
        identb_s = sb("identb_s", [128, 128], BF16)
        xse = sb("xse", [128, 1024], BF16)
        xsp = sb("xsp", [128, 1024], BF16)
        xsT = sb("xsT", [128, 3072], BF16)
        w1e_s = sb("w1e_s", [128, 4096], BF16)
        w2e_s = sb("w2e_s", [128, 4096], BF16)
        geT = sb("geT", [128, 1536], BF16)
        yeb = sb("yeb", [128, 1024], BF16)
        x_t = sb("x_t", [128, 1024], F32)
        scr1 = sb("scr1", [128, 512], F32)
        scr2 = sb("scr2", [128, 512], F32)
        psA = ctx.enter_context(nc.psum_tensor("psA", [128, 512], F32))
        psTb = ctx.enter_context(nc.psum_tensor("psTb", [128, 128], BF16))
        g = ctx.enter_context(nc.semaphore("g"))
        blk = ctx.enter_context(nc.Block())
        p = P()
        p.dma(identb_s[:], identb[:, :])
        for e in range(64):
            for dt in range(8):
                p.dma(x_t[:, 0:512], w1e[1024 * e + 128 * dt: 1024 * e + 128 * (dt + 1), :])
                p.op('vector', 'tensor_copy', w1e_s[:, 512 * dt:512 * (dt + 1)], x_t[:, 0:512])
            for mt in range(3):
                p.dma(xse[:], XSp[0][384 * e + 128 * mt: 384 * e + 128 * (mt + 1), :])
                for j in range(1, npart):
                    p.dma(xsp[:], XSp[j][384 * e + 128 * mt: 384 * e + 128 * (mt + 1), :])
                    p.op('vector', 'tensor_tensor', xse[:], xse[:], xsp[:], op=ALU.add)
                for dt in range(8):
                    p.op('tensor', 'transpose', psTb[:], xse[:, 128 * dt:128 * (dt + 1)], identb_s[:])
                    p.op('scalar', 'copy', xsT[:, 384 * dt + 128 * mt: 384 * dt + 128 * (mt + 1)],
                         psTb[:])
            for mt in range(4):
                for dt in range(8):
                    p.op('tensor', 'matmul', psA[:, 0:384],
                         w1e_s[:, 512 * dt + 128 * mt: 512 * dt + 128 * (mt + 1)],
                         xsT[:, 384 * dt:384 * (dt + 1)], start=(dt == 0), stop=(dt == 7))
                emit_gelu_tanh(p, psA[:, 0:384], geT[:, 384 * mt:384 * (mt + 1)],
                               scr1[:, 0:384], scr2[:, 0:384], 384)
            for ft in range(4):
                p.dma(x_t[:], w2e[512 * e + 128 * ft: 512 * e + 128 * (ft + 1), :])
                p.op('vector', 'tensor_copy', w2e_s[:, 1024 * ft:1024 * (ft + 1)], x_t[:])
            for mt in range(3):
                for n in range(2):
                    for ft in range(4):
                        p.op('tensor', 'matmul', psA[:, :],
                             geT[:, 384 * ft + 128 * mt: 384 * ft + 128 * (mt + 1)],
                             w2e_s[:, 1024 * ft + 512 * n: 1024 * ft + 512 * (n + 1)],
                             start=(ft == 0), stop=(ft == 3))
                    p.op('vector', 'tensor_copy', yeb[:, 512 * n:512 * (n + 1)], psA[:, :])
                p.dma(YS[384 * e + 128 * mt: 384 * e + 128 * (mt + 1), :], yeb[:])
        p.build(nc, g, blk)
    return nc


def build_l1_gath(tiles):
    nc = bass.Bass("TRN2", target_bir_lowering=False)
    YS = nc.dram_tensor("YS", [NSLOT + 1, D], BF16, kind="ExternalInput").ap()
    slotsi = nc.dram_tensor("slotsi", [128, 192], U32, kind="ExternalInput").ap()
    wsi = nc.dram_tensor("wsi", [128, 192], F32, kind="ExternalInput").ap()
    ysh = nc.dram_tensor("ysh", [R, D], F32, kind="ExternalInput").ap()
    y = nc.dram_tensor("y", [len(tiles) * 128, D], F32, kind="ExternalOutput").ap()
    ctx = contextlib.ExitStack()
    with ctx:
        sb = lambda name, shape, dt: ctx.enter_context(nc.sbuf_tensor(name, shape, dt))
        slots = sb("slots", [128, 192], U32)
        ws = sb("ws", [128, 192], F32)
        gtile = sb("gtile", [128, 1024], BF16)
        gacc = sb("gacc", [128, 1024], F32)
        scr1 = sb("scr1", [128, 512], F32)
        scr2 = sb("scr2", [128, 512], F32)
        x_t = sb("x_t", [128, 1024], F32)
        g = ctx.enter_context(nc.semaphore("g"))
        blk = ctx.enter_context(nc.Block())
        p = P()
        p.dma(slots[:], slotsi[:, :])
        p.dma(ws[:], wsi[:, :])
        for oi, i in enumerate(tiles):
            p.op('vector', 'memset', gacc[:], 0.0)
            for k in range(6):
                p.op('vector', 'memset', gtile[:], 0.0)
                p.st('gpsimd', lambda e, col=slots[:, 6 * i + k:6 * i + k + 1]: e.indirect_dma_start(
                    out=gtile[:, :], out_offset=None, in_=YS[:, :], in_offset=IOA(col, 0),
                    bounds_check=NSLOT - 1, oob_is_err=False), 16)
                p.op('vector', 'tensor_scalar', scr1[:], gtile[:, 0:512],
                     scalar1=ws[:, 6 * i + k:6 * i + k + 1], scalar2=None, op0=ALU.mult)
                p.op('vector', 'tensor_scalar', scr2[:], gtile[:, 512:1024],
                     scalar1=ws[:, 6 * i + k:6 * i + k + 1], scalar2=None, op0=ALU.mult)
                p.op('vector', 'tensor_tensor', gacc[:, 0:512], gacc[:, 0:512], scr1[:], op=ALU.add)
                p.op('vector', 'tensor_tensor', gacc[:, 512:1024], gacc[:, 512:1024], scr2[:],
                     op=ALU.add)
            p.dma(x_t[:], ysh[128 * i:128 * (i + 1), :])
            p.op('vector', 'tensor_tensor', gacc[:], gacc[:], x_t[:], op=ALU.add)
            p.dma(y[128 * oi:128 * (oi + 1), :], gacc[:])
        p.build(nc, g, blk)
    return nc


def l1_ffn_inmaps_prep(inputs, x3full, asgT_all):
    import ml_dtypes
    strup = np.triu(np.ones((128, 128), np.float32), 1)
    common = {
        "rw": inputs["L1_ffn_router_weight"],
        "ebias": inputs["L1_ffn_expert_bias"].reshape(1, -1),
        "sw1": inputs["L1_ffn_sw1"],
        "sw2": inputs["L1_ffn_sw2"],
        "identf": np.eye(128, dtype=np.float32),
        "identb": np.eye(128, dtype=np.float32).astype(ml_dtypes.bfloat16),
        "onesr": np.ones((1, 128), np.float32),
        "onesc": np.ones((128, 1), np.float32),
        "iotac": np.arange(128, dtype=np.float32).reshape(128, 1),
        "iotee": np.tile(np.arange(64, dtype=np.float32)[None, :], (128, 1)),
        "strup": strup,
    }
    maps = []
    for c in range(NCORES):
        m = dict(common)
        m["x"] = np.ascontiguousarray(x3full[c * R:(c + 1) * R])
        prev = np.zeros((64, 28672), np.float32)
        if c > 0:
            prev[:, :c * R] = asgT_all[:, :c * R]
        m["asgF"] = prev
        maps.append(m)
    return maps


SCAT_GROUPS = [list(range(7 * j, min(7 * j + 7, 32))) for j in range(5)]
